# version 03
# baseline (speedup 1.0000x reference)
import math
import numpy as np
import ml_dtypes

import concourse.bass as bass
import concourse.mybir as mybir
from concourse.bass_utils import run_bass_kernel_spmd
from concourse.alu_op_type import AluOpType as OP

F32 = mybir.dt.float32
BF16 = mybir.dt.bfloat16
I32 = mybir.dt.int32
AF = mybir.ActivationFunctionType

NEG = -3.0e38
NEG8 = -3.75e37


class Cfg:
    def __init__(self, D, INNER, HEADS, W, L, HID, NB):
        self.D = D
        self.NKC = D // 128
        self.INNER = INNER
        self.NQ = INNER // 128
        self.HEADS = HEADS
        self.DH = INNER // HEADS
        self.HPC = 128 // self.DH
        self.W = W
        self.TPW = W // 128
        self.L = L
        self.NW = L // W
        self.NT = L // 128
        self.HID = HID
        self.NH = (HID + 127) // 128
        self.NB = NB
        self.BAND = W + 128
        self.OUTT0 = self.NT // 2
        s1_need = (L // 2) - (W + 1)
        self.B1T0 = s1_need // 128
        self.A1T0 = self.B1T0 - 1
        self.A1WLO = max((self.A1T0 * 128) // W - 1, 0)
        self.A2WLO = (L // 2) // W - 1
        self.SIMT0 = max((self.A1WLO * W - 1) // 128, 0)
        self.NTS = self.NT - self.SIMT0
        self.simchunks = [((i * 128 + 127) // 512) + 1
                          for i in range(self.SIMT0, self.NT)]
        self.NSIMU = sum(self.simchunks)

        def chunks(t0, t1, step):
            out = []
            cc = t0 * 128
            while cc < t1 * 128:
                n = min(step, t1 * 128 - cc)
                out.append((cc, n))
                cc += n
            return out
        self.b1chunks = chunks(self.B1T0, self.NT, 336)
        self.b2chunks = chunks(self.OUTT0, self.NT, 336)
        self.a1qt = list(range(self.A1T0, self.NT))
        self.a2qt = list(range(self.OUTT0, self.NT))
        self.DEPTH = 2
        self.QK_SCALE = 8.0
        self.GHALF = L // 2


REAL = Cfg(D=1024, INNER=512, HEADS=8, W=512, L=4096, HID=2730, NB=4)
ANALOG = Cfg(D=256, INNER=128, HEADS=2, W=128, L=1024, HID=330, NB=4)



ENGS = ("sync", "gpsimd", "tensor", "vector", "scalar")


class Sched:
    def __init__(self):
        self.steps = {e: [] for e in ENGS}
        self.cnt = {e: 0 for e in ENGS}
        self.seen = {e: {} for e in ENGS}

    def op(self, eng, fn, after=()):
        waits = []
        for tok in after:
            if tok is None:
                continue
            peng, pval = tok
            if peng == eng:
                continue
            if self.seen[eng].get(peng, 0) >= pval:
                continue
            self.seen[eng][peng] = pval
            waits.append((peng, pval))
        self.cnt[eng] += 16 if eng in ("sync", "gpsimd") else 1
        self.steps[eng].append((waits, fn))
        return (eng, self.cnt[eng])

    def load(self, dst, src, after=(), eng="sync", slow=False):
        return self.op(eng, lambda e, d=dst, s=src, sl=slow: e.dma_start(
            out=d, in_=s, allow_slow_non_contiguous=sl), after)

    def store(self, dst, src, after=(), eng="gpsimd", slow=False):
        return self.op(eng, lambda e, d=dst, s=src, sl=slow: e.dma_start(
            out=d, in_=s, allow_slow_non_contiguous=sl), after)


def emit(nc, sched):
    cms = [nc.semaphore("sem_" + e) for e in ENGS]
    sems = {}
    for e, cm in zip(ENGS, cms):
        sems[e] = cm.__enter__()
    with nc.Block() as block:

        def make(engname):
            def body(eng):
                inc = 16 if engname in ("sync", "gpsimd") else 1
                n = 0
                for waits, fn in sched.steps[engname]:
                    for peng, pval in waits:
                        eng.wait_ge(sems[peng], pval)
                    inst = fn(eng)
                    if engname in ("tensor", "vector", "scalar"):
                        inst = eng.drain()
                    n += inc
                    inst.then_inc(sems[engname], inc)
                if engname == "gpsimd":
                    eng.wait_ge(sems["gpsimd"], n)
            return body

        block.sync(make("sync"))
        block.gpsimd(make("gpsimd"))
        block.tensor(make("tensor"))
        block.vector(make("vector"))
        block.scalar(make("scalar"))



def build_program(c, dbg=False):
    nc = bass.Bass("TRN2", target_bir_lowering=False)
    p = Sched()

    D, L, NKC, NQ, W, DH, HPC = c.D, c.L, c.NKC, c.NQ, c.W, c.DH, c.HPC
    HEADS, NH, HID, BAND = c.HEADS, c.NH, c.HID, c.BAND
    HALF = DH // 2
    CH0 = 256
    CHB = 336

    def din(name, shape, dt=F32):
        return nc.dram_tensor(name, list(shape), dt, kind="ExternalInput").ap()

    xT = din("xT", [NKC, 128, L])
    trow = din("trow", [1, L])
    simmask = din("simmask", [c.NSIMU, 128, 512], BF16)
    selmask = din("selmask", [128, c.NTS])
    attnM1 = din("attnM1", [len(c.a1qt), 128, BAND], BF16)
    attnM2 = din("attnM2", [len(c.a2qt), 128, BAND], BF16)
    emar = din("emar", [NKC, 128, 1])
    s_log_tau = din("s_log_tau", [1, 1])
    s_log_blend = din("s_log_blend", [1, 1])
    s_logit_blend_seq = din("s_logit_blend_seq", [1, 1])
    wqkv = din("wqkv", [c.DEPTH, D, 3 * c.INNER])
    wo = din("wo", [c.DEPTH, c.INNER, D])
    w1 = din("w1", [c.DEPTH, D, 2 * HID])
    w2 = din("w2", [c.DEPTH, HID, D])
    b1a_p = din("b1a_p", [c.DEPTH, 128, NH])
    b1g_p = din("b1g_p", [c.DEPTH, 128, NH])
    b2p = din("b2p", [c.DEPTH, 128, NKC])
    lnbg = din("lnbg", [c.DEPTH, NKC, 2, 128])
    qse = din("qse", [c.DEPTH, NQ, 8, 128])
    kse = din("kse", [c.DEPTH, NQ, 8, 128])
    h8pat = din("h8pat", [NQ, 128, 8], BF16)
    ones1 = din("ones1", [1, 128])
    onecol_bf = din("onecol_bf", [128, 1], BF16)
    identity = din("identity", [128, 128], BF16)
    identity32 = din("identity32", [128, 128])
    drow = din("drow", [128, 1])
    rotm = din("rotm", [128, 128])
    sel8 = din("sel8", [HEADS, 8, DH])
    onz = din("onz", [2, 128])
    onesrow = din("onesrow", [1, 512])

    KND = "ExternalOutput" if dbg else "Internal"
    OUTS = nc.dram_tensor("OUTS", [NKC, 128, L], F32, kind=KND).ap()
    XMID = nc.dram_tensor("XMID", [NKC, 128, L], F32, kind=KND).ap()
    S0 = nc.dram_tensor("S0", [NKC, 128, L], F32, kind=KND).ap()
    SA = nc.dram_tensor("SA", [NKC, 128, L], F32, kind=KND).ap()
    GF = nc.dram_tensor("GF", [c.NTS, 128], F32, kind=KND).ap()
    W2C = nc.dram_tensor("W2C", [c.DEPTH, c.NKC, 128, NH * 128], BF16,
                         kind="Internal").ap()
    OUT = nc.dram_tensor("OUT", [NKC, 128, L // 2], F32,
                         kind="ExternalOutput").ap()
    DBG = nc.dram_tensor("DBG", [128, 2 * c.NTS], F32, kind=KND).ap()
    DBG2 = nc.dram_tensor("DBG2", [128, 2048], F32, kind=KND).ap()

    sb_ctx = []

    def sbuf(name, shape, dt):
        cm = nc.sbuf_tensor(name, list(shape), dt)
        t = cm.__enter__()
        sb_ctx.append(cm)
        return t.ap() if hasattr(t, "ap") else t

    seq_tm = sbuf("seq_tm", [128, c.NTS], F32)
    ema_tm = sbuf("ema_tm", [128, c.NTS], F32)
    sel_sb = sbuf("sel_sb", [128, c.NTS], F32)
    ema_bf = sbuf("ema_bf", [128, NKC, 1], BF16)
    emastage = sbuf("emastage", [128, NKC], F32)
    scal = sbuf("scal", [1, 16], F32)
    scbc = sbuf("scbc", [128, 8], F32)
    ones1_sb = sbuf("ones1_sb", [1, 128], F32)
    onecol_sb = sbuf("onecol_sb", [128, 1], BF16)
    ident_sb = sbuf("ident_sb", [128, 128], BF16)
    ident32_sb = sbuf("ident32_sb", [128, 128], F32)
    h8_sb = sbuf("h8_sb", [128, NQ, 8], BF16)
    drow_sb = sbuf("drow_sb", [128, 1], F32)
    rotm_sb = sbuf("rotm_sb", [128, 128], F32)
    sel8_sb = sbuf("sel8_sb", [8, HEADS, DH], F32)
    statrow = sbuf("statrow", [2, 512], F32)
    onz_sb = sbuf("onz_sb", [2, 128], F32)
    stats = sbuf("stats", [1, 4 * 512], F32)
    lnbase = sbuf("lnbase", [128, 2], F32)
    cvals = sbuf("cvals", [128, 4], F32)
    b1a_sb = sbuf("b1a_sb", [128, NH], F32)
    b1g_sb = sbuf("b1g_sb", [128, NH], F32)
    b2_sb = sbuf("b2_sb", [128, NKC], F32)
    lnbg_sb = sbuf("lnbg_sb", [2, NKC, 128], F32)
    qse_sb = sbuf("qse_sb", [8, NQ, 128], F32)
    kse_sb = sbuf("kse_sb", [8, NQ, 128], F32)
    nrm8 = sbuf("nrm8", [8, 2 * 512], F32)
    sums8 = sbuf("sums8", [128, 2, 8], F32)
    rcp8 = sbuf("rcp8", [128, 2, 8], F32)
    rT = sbuf("rT", [8, 2, 128], F32)
    mx2 = sbuf("mx2", [128, 2, 4], F32)
    rtmpi = sbuf("rtmpi", [128, 512], I32)

    afP0 = 2 * NKC * CH0 + 2 * NKC * CH0 + NKC * CH0
    afP2 = 2 * NKC * CH0 + 2 * NKC * CH0 + NKC * CH0 + NKC * CH0 + CH0
    afA = (2 * NKC * (W + 1) + 3 * W + 2 * W + 4 * W + (W + 8) + 2 * BAND +
           3 * c.INNER // 2)
    afB = (HID + 8) + 2 * NKC * (CHB + 1) + NKC * CHB + NKC * CHB + 2 * CHB
    AFSZ = max(afP0, afP2, afA, afB) + 64
    abP0 = NKC * L + 2 * NKC * CH0
    abP1 = NKC * L + 2 * 512
    abA = (NKC * 3 * c.INNER + NQ * D + NKC * W + 2 * NQ * W +
           2 * (W // 128) * c.INNER + 6 * NQ * W + 2 * BAND + 2 * BAND +
           2 * BAND)
    abB = (NKC * 2 * HID + 2 * NH * 128 + NKC * CHB +
           max(NKC * CHB, NH * CHB))
    ABSZ = max(abP0, abP1, abA, abB) + 64
    ARF = sbuf("ARF", [128, AFSZ], F32)
    ARB = sbuf("ARB", [128, ABSZ], BF16)

    class Arena:
        def __init__(self, ap):
            self.ap = ap
            self.off = 0

        def take(self, cols):
            v = self.ap[:, self.off:self.off + cols]
            self.off += cols
            return v

        def take3(self, a, b):
            v = self.take(a * b)
            return v.rearrange("p (a b) -> p a b", b=b)

        def reset(self):
            self.off = 0

    af = Arena(ARF)
    ab = Arena(ARB)

    ps_ctx = []

    def psum(name, shape, dt=F32):
        cm = nc.psum_tensor(name, list(shape), dt)
        t = cm.__enter__()
        ps_ctx.append(cm)
        return t.ap() if hasattr(t, "ap") else t

    B0 = psum("B0", [128, 512])
    B1 = psum("B1", [128, 512])
    B2 = psum("B2", [128, 512])
    B3 = psum("B3", [128, 512])
    B4 = psum("B4", [128, 512])
    B5 = psum("B5", [128, 512])
    B6 = psum("B6", [128, 512])
    BT = psum("BT", [128, 1024], BF16)

    def mm(ps_ap, lhsT, rhs, start, stop, after=()):
        return p.op("tensor",
                    lambda e, o=ps_ap, a=lhsT, b=rhs, s1=start, s2=stop:
                    e.matmul(o, a, b, start=s1, stop=s2), after)

    def mm_acc(ps_ap, pairs, after=()):
        def fn(e):
            n = len(pairs)
            inst = None
            for i, (a, b) in enumerate(pairs):
                inst = e.matmul(ps_ap, a, b, start=(i == 0), stop=(i == n - 1))
            return inst
        return p.op("tensor", fn, after)

    def vop(fn, after=()):
        return p.op("vector", fn, after)

    def sop(fn, after=()):
        return p.op("scalar", fn, after)

    t_on = p.load(ones1_sb[:, :], ones1[:, :])
    p.load(onecol_sb[:, :], onecol_bf[:, :])
    p.load(ident_sb[:, :], identity[:, :])
    p.load(ident32_sb[:, :], identity32[:, :])
    t_h8 = p.load(h8_sb[:, :, :], h8pat.rearrange("a p b -> p a b"))
    p.load(drow_sb[:, :], drow[:, :])
    t_rotm = p.load(rotm_sb[:, :], rotm[:, :])
    t_sel = p.load(sel_sb[:, :], selmask[:, :])
    t_sel8 = p.load(sel8_sb[:, :, :], sel8.rearrange("a b c -> b a c"))
    t_onz = p.load(onz_sb[:, :], onz[:, :])
    t_eml = p.load(emastage[:, :], emar.rearrange("a p b -> p (a b)"),
                   slow=True)
    t_emb = sop(lambda e: e.activation(out=ema_bf[:, :, 0],
                                       in_=emastage[:, :], func=AF.Copy),
                [t_eml])
    t_l1 = p.load(scal[:, 0:1], s_log_tau[:, :])
    t_l2 = p.load(scal[:, 1:2], s_log_blend[:, :])
    t_l3 = p.load(scal[:, 2:3], s_logit_blend_seq[:, :])
    t_tau = sop(lambda e: e.activation(out=scal[:, 4:5], in_=scal[:, 0:1],
                                       func=AF.Exp), [t_l1])
    t_al = sop(lambda e: e.activation(out=scal[:, 5:6], in_=scal[:, 1:2],
                                      func=AF.Sigmoid), [t_l2])
    t_ws = sop(lambda e: e.activation(out=scal[:, 6:7], in_=scal[:, 2:3],
                                      func=AF.Sigmoid), [t_l3])
    t_nt = vop(lambda e: e.tensor_scalar(out=scal[:, 7:8], in0=scal[:, 4:5],
                                         scalar1=-1.0, scalar2=None,
                                         op0=OP.mult), [t_tau])
    t_oa = vop(lambda e: e.tensor_scalar(out=scal[:, 8:9], in0=scal[:, 5:6],
                                         scalar1=-1.0, scalar2=1.0,
                                         op0=OP.mult, op1=OP.add), [t_al])
    t_es = mm_acc(B5[0:1, 0:1],
                  [(ema_bf[:, kc, :], ema_bf[:, kc, :]) for kc in range(NKC)],
                  [t_emb])
    t_es2 = sop(lambda e: e.activation(out=scal[:, 10:11], in_=B5[0:1, 0:1],
                                       func=AF.Sqrt), [t_es])
    t_es3 = vop(lambda e: e.tensor_scalar(out=scal[:, 11:12],
                                          in0=scal[:, 10:11], scalar1=1e-12,
                                          scalar2=None, op0=OP.max), [t_es2])
    t_es4 = vop(lambda e: e.reciprocal(out=scal[:, 9:10],
                                       in_=scal[:, 11:12]), [t_es3])
    tb_toks = []
    for i, (srci, tok) in enumerate([(7, t_nt), (5, t_al), (8, t_oa),
                                     (6, t_ws), (9, t_es4)]):
        tb_toks.append(mm(B6[:, i:i + 1], ones1_sb[:, :],
                          scal[:, srci:srci + 1], True, True, [tok, t_on]))
    t_scbc = vop(lambda e: e.tensor_copy(out=scbc[:, 0:5], in_=B6[:, 0:5]),
                 tb_toks)
    t_srones = p.load(statrow[1:2, :], onesrow[:, :])
    t_cv = vop(lambda e: [e.memset(cvals[:, 0:1], 1e-5),
                          e.memset(cvals[:, 1:2], 0.0),
                          e.memset(cvals[:, 2:3], 0.0)][-1])

    def ln_sums(shifted, bf_buf, sq_buf, N, after, psum_free=()):
        t1 = sop(lambda e: [e.activation(out=bf_buf[:, kc, 0:N],
                                         in_=shifted[kc], func=AF.Copy)
                            for kc in range(NKC)][-1], after)
        t2 = sop(lambda e: [e.activation(out=sq_buf[:, kc, 0:N],
                                         in_=shifted[kc], func=AF.Square)
                            for kc in range(NKC)][-1], after)
        tS = mm_acc(B4[0:1, 0:N], [(onecol_sb[:, :], bf_buf[:, kc, 0:N])
                                   for kc in range(NKC)],
                    [t1] + list(psum_free))
        tQ = mm_acc(B5[0:1, 0:N], [(onecol_sb[:, :], sq_buf[:, kc, 0:N])
                                   for kc in range(NKC)],
                    [t2] + list(psum_free))
        invD = 1.0 / D
        tmu = vop(lambda e: e.tensor_scalar(out=stats[:, 0:N],
                                            in0=B4[0:1, 0:N], scalar1=invD,
                                            scalar2=None, op0=OP.mult), [tS])
        tx2 = vop(lambda e: e.tensor_scalar(out=stats[:, 512:512 + N],
                                            in0=B5[0:1, 0:N], scalar1=invD,
                                            scalar2=None, op0=OP.mult), [tQ])
        tv1 = vop(lambda e: e.scalar_tensor_tensor(
            out=stats[:, 1536:1536 + N], in0=stats[:, 0:N], scalar=-1.0,
            in1=stats[:, 0:N], op0=OP.mult, op1=OP.mult), [tmu])
        tv2 = vop(lambda e: e.tensor_tensor(
            out=stats[:, 1536:1536 + N], in0=stats[:, 512:512 + N],
            in1=stats[:, 1536:1536 + N], op=OP.add), [tv1, tx2])
        tsq = sop(lambda e: e.activation(out=stats[:, 1536:1536 + N],
                                         in_=stats[:, 1536:1536 + N],
                                         func=AF.Sqrt, bias=cvals[0:1, 0:1]),
                  [tv2, t_cv])
        trs = vop(lambda e: e.reciprocal(out=stats[:, 1024:1024 + N],
                                         in_=stats[:, 1536:1536 + N]), [tsq])
        tbl = vop(lambda e: e.scalar_tensor_tensor(
            out=statrow[0:1, 0:N], in0=stats[:, 0:N], scalar=-1.0,
            in1=stats[:, 1024:1024 + N], op0=OP.mult, op1=OP.mult),
            [trs, t_srones])
        return trs, tbl, t2

    def ln_apply(shifted, dst, tmpA, N, tok_stats, lhsA, lhsB, nrows, after):
        trs, tbl = tok_stats[0], tok_stats[1]
        toks = []
        prev = None
        for kc in range(NKC):
            def mmab(e, kc=kc):
                e.matmul(B2[:, 0:N], lhsA(kc), stats[:, 1024:1024 + N],
                         start=True, stop=True)
                return e.matmul(B3[:, 0:N], lhsB(kc), statrow[0:nrows, 0:N],
                                start=True, stop=True)
            tab = p.op("tensor", mmab,
                       [trs, tbl] + list(after) + ([prev] if prev else []))
            t1 = vop(lambda e, kc=kc: e.tensor_tensor(
                out=tmpA[:, kc, 0:N], in0=shifted[kc], in1=B2[:, 0:N],
                op=OP.mult), [tab])
            t2 = vop(lambda e, kc=kc: e.tensor_tensor(
                out=dst[:, kc, 0:N], in0=tmpA[:, kc, 0:N], in1=B3[:, 0:N],
                op=OP.add), [t1])
            prev = t2
            toks.append(t2)
        return toks

    af.reset()
    ab.reset()
    out_norm = ab.take3(NKC, L)
    bfA = ab.take3(NKC, CH0)
    bfB = ab.take3(NKC, CH0)
    xa = af.take3(NKC, 2 * CH0)
    outsb = af.take3(NKC, 2 * CH0)
    tmpA0 = af.take3(NKC, CH0)

    p0chunks = []
    t0 = 0
    while t0 < L:
        p0chunks.append((t0, min(CH0, L - t0)))
        t0 += CH0
    p0_done = []
    p0_laststore = None
    for ch, (t0c, N) in enumerate(p0chunks):
        buf = ch % 2
        xsl = xa[:, :, buf * CH0:buf * CH0 + N]
        osl = outsb[:, :, buf * CH0:buf * CH0 + N]
        lddep = [p0_done[ch - 2]] if ch >= 2 else [t_emb]
        tl = None
        for kc in range(NKC):
            tl = p.load(xsl[:, kc, :], xT[kc, :, t0c:t0c + N],
                        lddep if kc == 0 else [])
        src = [xsl[:, kc, :] for kc in range(NKC)]
        pf = [p0_done[ch - 1]] if ch >= 1 else [t_scbc]
        st = ln_sums(src, bfA, bfB, N, [tl], psum_free=pf)
        ap = ln_apply(src, osl, tmpA0, N, st, lambda kc: ones1_sb[:, :],
                      lambda kc: onz_sb[:, :], 2, [tl, t_onz] + pf)
        tgl = sop(lambda e, osl=osl: e.activation(
            out=osl[:, :, :], in_=osl[:, :, :], func=AF.Gelu_apprx_tanh), ap)
        tst = None
        for kc in range(NKC):
            tst = p.store(OUTS[kc, :, t0c:t0c + N], osl[:, kc, :], [tgl])
        p0_laststore = tst
        tsq2 = sop(lambda e, osl=osl, N=N: e.activation(
            out=bfB[:, :, 0:N], in_=osl[:, :, :], func=AF.Square), [tgl])
        tss = mm_acc(B4[0:1, 0:N], [(onecol_sb[:, :], bfB[:, kc, 0:N])
                                    for kc in range(NKC)], [tsq2])
        tnr = sop(lambda e, N=N: e.activation(out=stats[:, 1536:1536 + N],
                                              in_=B4[0:1, 0:N],
                                              func=AF.Sqrt), [tss])
        tcl = vop(lambda e, N=N: e.tensor_scalar(
            out=stats[:, 1536:1536 + N], in0=stats[:, 1536:1536 + N],
            scalar1=1e-12, scalar2=None, op0=OP.max), [tnr])
        trc = vop(lambda e, N=N: e.reciprocal(
            out=stats[:, 1024:1024 + N], in_=stats[:, 1536:1536 + N]), [tcl])
        tbc = mm(B2[:, 0:N], ones1_sb[:, :], stats[:, 1024:1024 + N],
                 True, True, [trc])
        tn = None
        for kc in range(NKC):
            tn = vop(lambda e, kc=kc, t0c=t0c, N=N, osl=osl: e.tensor_tensor(
                out=out_norm[:, kc, t0c:t0c + N], in0=osl[:, kc, :],
                in1=B2[:, 0:N], op=OP.mult), [tbc, tst])
        p0_done.append(tn)

    af.reset()
    ab.reset()
    out_norm_p1 = ab.take3(NKC, L)
    simmk = ab.take3(2, 512)
    sink = af.take(512)
    p1a = [p0_done[-1]]
    unit = 0
    mask_tok = [None, None]
    prev_tv = [None, None]
    prev_ema = None
    tile_done = []
    for ti in range(c.NTS):
        i = c.SIMT0 + ti
        tmax = None
        for cc in range(c.simchunks[ti]):
            par = unit % 2
            bb = B0 if par == 0 else B1
            mdep = [mask_tok[par]] if mask_tok[par] else []
            tml = p.load(simmk[:, par, :], simmask[unit, :, :], mdep + p1a)
            pairs = [(out_norm[:, kc, i * 128:(i + 1) * 128],
                      out_norm[:, kc, cc * 512:(cc + 1) * 512])
                     for kc in range(NKC)]
            tm = mm_acc(bb[:, :], pairs,
                        p1a + ([prev_tv[par]] if prev_tv[par] else []))
            tva = vop(lambda e, bb=bb, par=par: e.tensor_tensor(
                out=sink[:, :], in0=bb[:, :], in1=simmk[:, par, :],
                op=OP.add), [tm, tml])
            tv = vop(lambda e: e.reduce_max(out=mx2[:, 0, 0:1],
                                            in_=sink[:, :],
                                            axis=mybir.AxisListType.X),
                     [tva])
            mask_tok[par] = tv
            prev_tv[par] = tv
            if cc == 0:
                tmax = vop(lambda e, ti=ti: e.tensor_copy(
                    out=seq_tm[:, ti:ti + 1], in_=mx2[:, 0, 0:1]), [tv])
            else:
                tmax = vop(lambda e, ti=ti: e.tensor_tensor(
                    out=seq_tm[:, ti:ti + 1], in0=seq_tm[:, ti:ti + 1],
                    in1=mx2[:, 0, 0:1], op=OP.max), [tv, tmax])
            unit += 1
        te = mm_acc(B2[:, 0:1], [(out_norm[:, kc, i * 128:(i + 1) * 128],
                                  ema_bf[:, kc, :]) for kc in range(NKC)],
                    p1a + [t_emb] + ([prev_ema] if prev_ema else []))
        prev_ema = vop(lambda e, ti=ti: e.tensor_copy(
            out=ema_tm[:, ti:ti + 1], in_=B2[:, 0:1]), [te])
        tile_done += [tmax, prev_ema]

    tdbg1 = p.store(DBG[:, 0:c.NTS], seq_tm[:, :], tile_done)
    tdbg2 = p.store(DBG[:, c.NTS:2 * c.NTS], ema_tm[:, :], tile_done)
    gdep = tile_done + [t_scbc, t_sel, tdbg1, tdbg2]
    tg1 = vop(lambda e: e.tensor_scalar(out=seq_tm[:, :], in0=seq_tm[:, :],
                                        scalar1=-2.0, scalar2=None,
                                        op0=OP.max), gdep)
    tg2 = vop(lambda e: e.tensor_scalar(out=ema_tm[:, :], in0=ema_tm[:, :],
                                        scalar1=scbc[:, 4:5], scalar2=None,
                                        op0=OP.mult), gdep)
    tg3 = vop(lambda e: e.tensor_tensor(out=seq_tm[:, :], in0=seq_tm[:, :],
                                        in1=ema_tm[:, :], op=OP.subtract),
              [tg1, tg2])
    tg4 = vop(lambda e: e.tensor_tensor(out=seq_tm[:, :], in0=seq_tm[:, :],
                                        in1=sel_sb[:, :], op=OP.mult), [tg3])
    tg5 = vop(lambda e: e.tensor_scalar(out=seq_tm[:, :], in0=seq_tm[:, :],
                                        scalar1=scbc[:, 3:4], scalar2=None,
                                        op0=OP.mult), [tg4])
    tg6 = vop(lambda e: e.tensor_tensor(out=seq_tm[:, :], in0=seq_tm[:, :],
                                        in1=ema_tm[:, :], op=OP.add), [tg5])
    tg7 = sop(lambda e: e.activation(out=seq_tm[:, :], in_=seq_tm[:, :],
                                     func=AF.Exp, scale=scbc[:, 0:1]), [tg6])
    tg8 = vop(lambda e: e.tensor_scalar(out=seq_tm[:, :], in0=seq_tm[:, :],
                                        scalar1=scbc[:, 1:2],
                                        scalar2=scbc[:, 2:3], op0=OP.mult,
                                        op1=OP.add), [tg7])
    tgf = p.store(GF.rearrange("a b -> b a"), seq_tm[:, :], [tg8], slow=True)

    af.reset()
    ab.reset()
    bfA2 = ab.take3(NKC, CH0)
    bfB2 = ab.take3(NKC, CH0)
    xa2 = af.take3(NKC, 2 * CH0)
    outsb2 = af.take3(NKC, 2 * CH0)
    xmidb = af.take3(NKC, CH0)
    tmpA2 = af.take3(NKC, CH0)
    gch = af.take(CH0)[0:1, :]
    GFflat = GF.rearrange("a b -> (a b)")

    p2chunks = []
    t0 = c.SIMT0 * 128
    while t0 < L:
        p2chunks.append((t0, min(CH0, L - t0)))
        t0 += CH0
    p2_done = []
    p2_xmst = []
    p2_cast = []
    for ci, (t0c, N) in enumerate(p2chunks):
        buf = ci % 2
        xsl = xa2[:, :, buf * CH0:buf * CH0 + N]
        osl = outsb2[:, :, buf * CH0:buf * CH0 + N]
        lddep = [p2_done[ci - 2]] if ci >= 2 else [tg8]
        tl = tl2 = None
        for kc in range(NKC):
            tl = p.load(xsl[:, kc, :], xT[kc, :, t0c:t0c + N],
                        lddep if kc == 0 else [])
            tl2 = p.load(osl[:, kc, :], OUTS[kc, :, t0c:t0c + N],
                         [p0_laststore] if (ci == 0 and kc == 0) else [])
        goff = t0c - c.SIMT0 * 128
        pf = [p2_done[ci - 1]] if ci >= 1 else [tg8]
        tgld = p.load(gch[:, 0:N], GFflat[None, goff:goff + N],
                      [tgf] + (pf if ci >= 1 else []))
        tgb = mm(B6[:, 0:N], ones1_sb[:, :], gch[:, 0:N],
                 True, True, [tgld, tl2] + pf)
        xm_guard = ([p2_xmst[ci - 1], p2_cast[ci - 1]]
                    if ci >= 1 else [])
        txm = []
        for kc in range(NKC):
            t1 = vop(lambda e, kc=kc, N=N, osl=osl: e.tensor_tensor(
                out=xmidb[:, kc, 0:N], in0=osl[:, kc, :], in1=B6[:, 0:N],
                op=OP.mult), [tgb, tl, tl2] + xm_guard)
            t2 = vop(lambda e, kc=kc, N=N, xsl=xsl: e.tensor_tensor(
                out=xmidb[:, kc, 0:N], in0=xmidb[:, kc, 0:N],
                in1=xsl[:, kc, :], op=OP.add), [t1])
            txm.append(t2)
        tst = None
        for kc in range(NKC):
            tst = p.store(XMID[kc, :, t0c:t0c + N], xmidb[:, kc, 0:N],
                          [txm[kc]])
        p2_xmst.append(tst)
        src = [xmidb[:, kc, 0:N] for kc in range(NKC)]
        st = ln_sums(src, bfA2, bfB2, N, txm, psum_free=pf)
        p2_cast.append(st[2])
        if dbg and ci == 2:
            tdd = p.store(DBG2[0:1, 0:4 * 512], stats[:, :], [st[0], st[1]])
            tdd2 = p.store(DBG2[1:2, 0:512], statrow[0:1, :], [tdd])
            tdd3 = p.store(DBG2[2:3, 0:256],
                           xmidb[0:1, 0, 0:256], [tdd2])
            st = (st[0], tdd3, st[2])
        ap = ln_apply(src, osl, tmpA2, N, st, lambda kc: ones1_sb[:, :],
                      lambda kc: onz_sb[:, :], 2, txm + [t_onz] + pf)
        ts0 = None
        for kc in range(NKC):
            ts0 = p.store(S0[kc, :, t0c:t0c + N], osl[:, kc, :], [ap[kc]])
        p2_done.append(ts0)

    def a_phase(layer, wlo, qtiles, attnM, src, dst, after):
        af.reset()
        ab.reset()
        wqkv_bf = ab.take3(NKC, 3 * c.INNER)
        wo_bf = ab.take3(NQ, D)
        hq_bf = ab.take3(NKC, W)
        kring = ab.take3(2, NQ * W).rearrange("p a (b c) -> p a b c", c=W)
        vtring = ab.take3(2, (W // 128) * c.INNER).rearrange(
            "p a (b c) -> p a b c", c=c.INNER)
        vbf = ab.take3(NQ, W)
        qbf = ab.take3(NQ, W)
        qstage = ab.take3(NQ, W)
        kstage = ab.take3(NQ, W)
        qsq = ab.take3(NQ, W)
        pbf = ab.take3(2, BAND)
        ptb2 = ab.take(2 * BAND)
        maskt = ab.take3(2, BAND)
        aout = ab.take3(NQ, W)
        swin = af.take3(NKC, 2 * (W + 1))
        qn_t = af.take(W)
        uv2 = af.take(max(2 * W, HEADS * 128))
        u_t = uv2[:, 0:W]
        v_t = uv2[:, W:2 * W]
        rbc_sb = uv2[:, 0:HEADS * 128]
        rS1 = af.take(W)
        rS2 = af.take(W)
        tabQC = af.take(W)
        tabQS = af.take(W)
        tabKC = af.take(W)
        tabKS = af.take(W)
        trw = af.take(W + 8)[0:1, :]
        Sm2 = af.take(2 * BAND)
        wstage = af.take(3 * c.INNER // 2)

        toks_w = []
        prev = list(after)
        for kc in range(NKC):
            for hh in range(2):
                hw = 3 * c.INNER // 2
                tl = p.load(wstage[:, :], wqkv[layer, kc * 128:(kc + 1) * 128,
                                              hh * hw:(hh + 1) * hw], prev)
                prev = [sop(lambda e, kc=kc, hh=hh, hw=hw: e.activation(
                    out=wqkv_bf[:, kc, hh * hw:(hh + 1) * hw],
                    in_=wstage[:, :], func=AF.Copy), [tl])[0:2]] if False                     else [sop(lambda e, kc=kc, hh=hh, hw=hw: e.activation(
                        out=wqkv_bf[:, kc, hh * hw:(hh + 1) * hw],
                        in_=wstage[:, :], func=AF.Copy), [tl])]
                prev = [prev[0]]
                toks_w.append(prev[0])
        for kc in range(NQ):
            tl = p.load(wstage[:, 0:D // 2], wo[layer, kc * 128:(kc + 1) * 128,
                                                0:D // 2], prev)
            prev = [sop(lambda e, kc=kc: e.activation(
                out=wo_bf[:, kc, 0:D // 2], in_=wstage[:, 0:D // 2],
                func=AF.Copy), [tl])]
            toks_w.append(prev[0])
            tl = p.load(wstage[:, 0:D // 2],
                        wo[layer, kc * 128:(kc + 1) * 128, D // 2:D], prev)
            prev = [sop(lambda e, kc=kc: e.activation(
                out=wo_bf[:, kc, D // 2:D], in_=wstage[:, 0:D // 2],
                func=AF.Copy), [tl])]
            toks_w.append(prev[0])
        toks_s = load_scales(layer, after)

        LN1E4_HI = 9.2109375
        LN1E4_LO = 9.340371861794704e-4
        TWO_PI_HI = 6.28125
        TWO_PI_LO = 3.1853071795864774e-3
        INV2PI = 0.15915494309189535
        PI_CL = 3.141592
        t0s = vop(lambda e: [e.memset(cvals[:, 1:2], -LN1E4_HI / HALF),
                             e.memset(cvals[:, 2:3], -LN1E4_LO / HALF)][-1],
                  list(after) + [t_cv])
        ti1 = sop(lambda e: e.activation(out=lnbase[:, 0:1], in_=drow_sb[:, :],
                                         func=AF.Exp, scale=cvals[:, 1:2]),
                  list(after) + [t0s])
        ti2 = sop(lambda e: e.activation(out=lnbase[:, 1:2], in_=drow_sb[:, :],
                                         func=AF.Exp, scale=cvals[:, 2:3]),
                  [ti1])
        ti3 = vop(lambda e: e.tensor_tensor(out=lnbase[:, 0:1],
                                            in0=lnbase[:, 0:1],
                                            in1=lnbase[:, 1:2], op=OP.mult),
                  [ti2])
        ti4 = vop(lambda e: e.tensor_scalar(out=lnbase[:, 1:2],
                                            in0=drow_sb[:, :],
                                            scalar1=2.0 / (1.4 * DH),
                                            scalar2=0.4 / 1.4, op0=OP.mult,
                                            op1=OP.add), [ti3])
        ti5 = sop(lambda e: e.activation(out=lnbase[:, 1:2],
                                         in_=lnbase[:, 1:2], func=AF.Ln),
                  [ti4])
        rope_init = [ti3, ti5]

        def build_rope_win(t0w, rdeps):
            Ph = u_t
            Qh = v_t
            ttl = p.load(trw[:, 0:W], trow[:, t0w:t0w + W], rdeps)
            tb = mm(B6[:, 0:W], ones1_sb[:, :], trw[:, 0:W],
                    True, True, [ttl] + rope_init + rdeps)
            tp = vop(lambda e: e.tensor_scalar(
                out=Qh, in0=B6[:, 0:W], scalar1=1.0 / (W // 2),
                scalar2=-float(c.GHALF) / (W // 2), op0=OP.mult, op1=OP.add),
                [tb] + rdeps)
            tsc = sop(lambda e: e.activation(out=rS1, in_=Qh, func=AF.Exp,
                                             scale=lnbase[:, 1:2]),
                      [tp] + rdeps)
            tpn = vop(lambda e: e.tensor_scalar(
                out=Qh, in0=Qh, scalar1=-1.0, scalar2=None, op0=OP.mult),
                [tsc])
            tsn = sop(lambda e: e.activation(out=rS2, in_=Qh, func=AF.Exp,
                                             scale=lnbase[:, 1:2]),
                      [tpn] + rdeps)
            tf = vop(lambda e: e.tensor_scalar(
                out=Ph, in0=B6[:, 0:W], scalar1=lnbase[:, 0:1],
                scalar2=None, op0=OP.mult), [tb, tsn])
            tn1 = vop(lambda e: e.tensor_scalar(
                out=Qh, in0=Ph, scalar1=INV2PI, scalar2=0.5, op0=OP.mult,
                op1=OP.add), [tf, tsn])
            tn2 = vop(lambda e: e.tensor_copy(out=rtmpi[:, 0:W], in_=Qh),
                      [tn1])
            tn3 = vop(lambda e: e.tensor_copy(out=Qh, in_=rtmpi[:, 0:W]),
                      [tn2])
            tr1 = vop(lambda e: e.scalar_tensor_tensor(
                out=Ph, in0=Qh, scalar=-TWO_PI_HI, in1=Ph, op0=OP.mult,
                op1=OP.add), [tn3])
            tr2 = vop(lambda e: e.scalar_tensor_tensor(
                out=Ph, in0=Qh, scalar=-TWO_PI_LO, in1=Ph, op0=OP.mult,
                op1=OP.add), [tr1])
            def wraps(e):
                e.tensor_scalar(out=qn_t[:, 0:W], in0=Ph, scalar1=math.pi,
                                scalar2=None, op0=OP.is_gt)
                e.scalar_tensor_tensor(out=Qh, in0=qn_t[:, 0:W],
                                       scalar=-2 * math.pi, in1=Ph,
                                       op0=OP.mult, op1=OP.add)
                e.tensor_scalar(out=qn_t[:, 0:W], in0=Qh, scalar1=-math.pi,
                                scalar2=None, op0=OP.is_lt)
                e.scalar_tensor_tensor(out=Qh, in0=qn_t[:, 0:W],
                                       scalar=2 * math.pi, in1=Qh,
                                       op0=OP.mult, op1=OP.add)
                e.tensor_scalar(out=Ph, in0=Qh, scalar1=math.pi / 2,
                                scalar2=None, op0=OP.add)
                e.tensor_scalar(out=qn_t[:, 0:W], in0=Ph, scalar1=math.pi,
                                scalar2=None, op0=OP.is_gt)
                return e.scalar_tensor_tensor(out=Ph, in0=qn_t[:, 0:W],
                                              scalar=-2 * math.pi, in1=Ph,
                                              op0=OP.mult, op1=OP.add)
            tws = vop(wraps, [tr2])
            twc = tws
            tcs = vop(lambda e: e.tensor_scalar(
                out=Qh, in0=Qh, scalar1=PI_CL, scalar2=-PI_CL, op0=OP.min,
                op1=OP.max), [twc])
            tcc = vop(lambda e: e.tensor_scalar(
                out=Ph, in0=Ph, scalar1=PI_CL, scalar2=-PI_CL, op0=OP.min,
                op1=OP.max), [tcs])
            tsin = sop(lambda e: e.activation(out=Qh, in_=Qh, func=AF.Sin),
                       [tcs, tsn])
            tcos = sop(lambda e: e.activation(out=Ph, in_=Ph, func=AF.Sin),
                       [tcc, tsin])
            tqc = vop(lambda e: e.tensor_tensor(out=tabQC, in0=Ph, in1=rS1,
                                                op=OP.mult), [tcos])
            tkc = vop(lambda e: e.tensor_tensor(out=tabKC, in0=Ph, in1=rS2,
                                                op=OP.mult), [tqc])
            tqs = vop(lambda e: e.tensor_tensor(out=tabQS, in0=Qh, in1=rS1,
                                                op=OP.mult), [tkc])
            tks = vop(lambda e: e.tensor_tensor(out=tabKS, in0=Qh, in1=rS2,
                                                op=OP.mult), [tqs])
            return tks

        qt_by_w = {}
        for qt in qtiles:
            qt_by_w.setdefault((qt * 128) // W, []).append(qt)
        win_cons = [None, None]
        prev_qkv = {0: None, 1: None}
        amask_tok = [None, None]
        up_te = [None, None]
        up_tpe = [None, None]
        up_tpv = [None, None]
        up_ttp = [None, None]
        qp = [dict(tcl=None, trt=None, tev=None) for _ in range(2)]
        last_tpe_win = None
        last_store = None
        prev_rope_cons = None
        qidx = 0
        unit2 = 0
        wdeps = [toks_w[-1], toks_s[-1]]
        for wl in range(wlo, c.NW):
            slot = wl % 2
            t0w = wl * W
            wdep = [win_cons[slot]] if win_cons[slot] else []
            sw = swin[:, :, slot * (W + 1):(slot + 1) * (W + 1)]
            tl = None
            if t0w == 0:
                tz = vop(lambda e, sw=sw: e.memset(sw[:, :, 0:1], 0.0),
                         wdep + list(after))
                for kc in range(NKC):
                    tl = p.load(sw[:, kc, 1:1 + W], src[kc, :, 0:W],
                                ([tz] + wdep + list(after) + wdeps)
                                if kc == 0 else [])
            else:
                for kc in range(NKC):
                    tl = p.load(sw[:, kc, :], src[kc, :, t0w - 1:t0w + W],
                                (wdep + list(after) + wdeps)
                                if kc == 0 else [])
            trop = build_rope_win(t0w, [prev_rope_cons] if prev_rope_cons
                                  else list(after))

            def hcast(e, sw=sw):
                e.activation(out=hq_bf[:, 0:NKC // 2, :],
                             in_=sw[:, 0:NKC // 2, 1:1 + W], func=AF.Copy)
                return e.activation(out=hq_bf[:, NKC // 2:NKC, :],
                                    in_=sw[:, NKC // 2:NKC, 0:W],
                                    func=AF.Copy)
            th = sop(hcast, [tl])
            ev = []
            for m in range(3 * NQ):
                par = m % 2
                bb = B0 if par == 0 else B1
                dep = [th, toks_w[-1]] + ([prev_qkv[par]] if prev_qkv[par]
                                          else [])
                tm = mm_acc(bb[:, 0:W],
                            [(wqkv_bf[:, kc, m * 128:(m + 1) * 128],
                              hq_bf[:, kc, :]) for kc in range(NKC)], dep)
                if m < NQ:
                    te_ = vop(lambda e, bb=bb, m=m: e.tensor_copy(
                        out=qstage[:, m, :], in_=bb[:, 0:W]), [tm])
                elif m < 2 * NQ:
                    te_ = vop(lambda e, bb=bb, m=m: e.tensor_copy(
                        out=kstage[:, m - NQ, :], in_=bb[:, 0:W]), [tm])
                else:
                    te_ = vop(lambda e, bb=bb, m=m: e.tensor_copy(
                        out=vbf[:, m - 2 * NQ, :], in_=bb[:, 0:W]), [tm])
                prev_qkv[par] = te_
                ev.append(te_)
            tvt = []
            tt = None
            for jc in range(W // 128):
                def vtall(e, jc=jc):
                    inst = None
                    for fc in range(NQ):
                        inst = e.transpose(
                            BT[:, fc * 128:(fc + 1) * 128],
                            vbf[:, fc, jc * 128:(jc + 1) * 128],
                            ident_sb[:, :])
                    return inst
                tt = p.op("tensor", vtall,
                          ev[2 * NQ:] + ([tt] if tt else
                                         ([last_tpe_win] if last_tpe_win
                                          else [])))
                tvv = vop(lambda e, jc=jc, slot=slot: e.tensor_copy(
                    out=vtring[:, slot, jc, :], in_=BT[:, 0:c.INNER]), [tt])
                tvt.append(tvv)
                tt = tvv

            def norm_rope(xs, dst4, Esb, tabC, tabS, sdep):
                t2 = sop(lambda e: e.activation(out=qsq[:, :, :],
                                                in_=xs[:, :, :],
                                                func=AF.Square), sdep)
                tq = mm_acc(B2[0:8, 0:W], [(h8_sb[:, cq, :], qsq[:, cq, :])
                                           for cq in range(NQ)],
                            [t2, t_h8])
                tn = sop(lambda e: e.activation(out=nrm8[:, 0:W],
                                                in_=B2[0:8, 0:W],
                                                func=AF.Sqrt), [tq])
                tc2 = vop(lambda e: e.tensor_scalar(
                    out=nrm8[:, 0:W], in0=nrm8[:, 0:W],
                    scalar1=1e-12, scalar2=None, op0=OP.max), [tn])
                tr = vop(lambda e: e.reciprocal(out=nrm8[:, 512:512 + W],
                                                in_=nrm8[:, 0:W]), [tc2])
                outs = []
                tb = None
                trot_p = None
                for cq in range(NQ):
                    tb = mm(B3[:, 0:W], Esb[:, cq, :], nrm8[:, 512:512 + W],
                            True, True, [tr, toks_s[-1]] +
                            ([tb] if tb else []))
                    t4 = vop(lambda e, cq=cq, xs=xs: e.tensor_tensor(
                        out=qn_t[:, 0:W], in0=xs[:, cq, :], in1=B3[:, 0:W],
                        op=OP.mult), [tb, trop])
                    trot = mm(B6[:, 0:W], rotm_sb[:, :], qn_t[:, 0:W],
                              True, True,
                              [t4, t_rotm] + ([trot_p] if trot_p else []))
                    trot_p = None
                    t5a = vop(lambda e: e.tensor_tensor(
                        out=u_t[:, 0:W], in0=qn_t[:, 0:W], in1=tabC,
                        op=OP.mult), [t4])
                    t5b = vop(lambda e: e.tensor_tensor(
                        out=v_t[:, 0:W], in0=B6[:, 0:W], in1=tabS,
                        op=OP.mult), [trot, t5a])
                    t5 = vop(lambda e, cq=cq, dst4=dst4: e.tensor_tensor(
                        out=dst4[:, cq, :], in0=u_t[:, 0:W],
                        in1=v_t[:, 0:W], op=OP.add), [t5b])
                    trot_p = t5b
                    outs.append(t5)
                return outs
            tqr = norm_rope(qstage, qbf, qse_sb, tabQC, tabQS, [ev[NQ - 1]])
            tkr = norm_rope(kstage, kring[:, slot, :, :], kse_sb, tabKC,
                            tabKS, [ev[2 * NQ - 1], tqr[-1]])
            ring_tok = tkr[-1]
            prev_rope_cons = ring_tok

            my_qt = qt_by_w.get(wl, [])
            wo_dep = []
            for qt in my_qt:
                qt_rel = qt * 128 - t0w
                mpar = qidx % 2
                qpar = qidx % 2
                mdep = [amask_tok[mpar]] if amask_tok[mpar] else []
                tml = p.load(maskt[:, mpar, :], attnM[qidx, :, :], mdep)
                head_toks = []
                first_pv_guard = ([qp[qpar]["tev"]] if qp[qpar]["tev"]
                                  else [])
                for h in range(HEADS):
                    par = unit2 % 2
                    hc = h // HPC
                    hoff = (h % HPC) * DH
                    lhs = qbf[hoff:hoff + DH, hc, qt_rel:qt_rel + 128]
                    kb0 = qt * 128 - W

                    def ksrc(tok0, n):
                        segs = []
                        t = tok0
                        left = n
                        while left > 0:
                            sl = (t // W) % 2
                            rel = t % W
                            nn = min(left, W - rel)
                            segs.append((sl, rel, nn))
                            t += nn
                            left -= nn
                        return segs
                    nA = min(512, BAND)
                    segsA = ksrc(kb0, nA)
                    segsB = ksrc(kb0 + nA, BAND - nA) if BAND > nA else []

                    def qk(e, lhs=lhs, segsA=segsA, segsB=segsB, hc=hc,
                           hoff=hoff):
                        inst = None
                        off = 0
                        for sl, rel, nn in segsA:
                            inst = e.matmul(
                                B0[:, off:off + nn], lhs,
                                kring[hoff:hoff + DH, sl, hc, rel:rel + nn],
                                start=True, stop=True)
                            off += nn
                        off = 0
                        for sl, rel, nn in segsB:
                            inst = e.matmul(
                                B1[:, off:off + nn], lhs,
                                kring[hoff:hoff + DH, sl, hc, rel:rel + nn],
                                start=True, stop=True)
                            off += nn
                        return inst
                    tqk = p.op("tensor", qk,
                               [ring_tok, tml] + head_toks[-1:] +
                               ([up_te[par]] if up_te[par] else []) +
                               first_pv_guard)
                    nB = BAND - nA

                    def ttrf(e, nA=nA, nB=nB, mpar=mpar, par=par):
                        Smp = Sm2[:, par * BAND:par * BAND + BAND]
                        inst = e.scalar_tensor_tensor(
                            out=Smp[:, 0:nA], in0=B0[:, 0:nA],
                            scalar=c.QK_SCALE, in1=maskt[:, mpar, 0:nA],
                            op0=OP.mult, op1=OP.add)
                        if nB:
                            inst = e.scalar_tensor_tensor(
                                out=Smp[:, nA:nA + nB], in0=B1[:, 0:nB],
                                scalar=c.QK_SCALE,
                                in1=maskt[:, mpar, nA:nA + nB],
                                op0=OP.mult, op1=OP.add)
                        return inst
                    tt1 = vop(ttrf, [tqk, tml] +
                              ([up_te[par]] if up_te[par] else []))
                    tt2 = vop(lambda e, par=par: e.reduce_max(
                        out=mx2[:, par, 2:3],
                        in_=Sm2[:, par * BAND:par * BAND + BAND],
                        axis=mybir.AxisListType.X), [tt1])
                    tt3 = vop(lambda e, par=par: e.tensor_scalar(
                        out=mx2[:, par, 3:4], in0=mx2[:, par, 2:3],
                        scalar1=-1.0, scalar2=None, op0=OP.mult), [tt2])
                    te2 = sop(lambda e, h=h, par=par, qpar=qpar:
                              e.activation(out=pbf[:, par, :],
                                           in_=Sm2[:, par * BAND:
                                                   par * BAND + BAND],
                                           func=AF.Exp,
                                           bias=mx2[:, par, 3:4],
                                           accum_out=sums8[:, qpar,
                                                           h:h + 1]),
                              [tt3] + ([up_ttp[par]] if up_ttp[par]
                                       else []) +
                              ([qp[qpar]["tcl"]] if qp[qpar]["tcl"]
                               else []))
                    def tpall(e, par=par):
                        inst = None
                        for jc in range(BAND // 128):
                            inst = e.transpose(
                                BT[:, jc * 128:(jc + 1) * 128],
                                pbf[:, par, jc * 128:(jc + 1) * 128],
                                ident_sb[:, :])
                        return inst
                    ttp = p.op("tensor", tpall,
                               [te2] + ([up_tpe[1 - par]] if up_tpe[1 - par]
                                        else []))
                    tpe = sop(lambda e, par=par: e.activation(
                        out=ptb2[:, par * BAND:par * BAND + BAND],
                        in_=BT[:, 0:BAND], func=AF.Copy),
                        [ttp] + ([up_tpv[par]] if up_tpv[par] else []))

                    def pv(e, qt=qt, h=h, par=par):
                        inst = None
                        nchb = BAND // 128
                        ptbp = ptb2[:, par * BAND:par * BAND + BAND]
                        for jc in range(nchb):
                            tok0 = qt * 128 - W + jc * 128
                            sl = (tok0 // W) % 2
                            jcc = (tok0 % W) // 128
                            ob = B2 if h < 4 else B3
                            inst = e.matmul(
                                ob[0:DH, (h % 4) * 128:(h % 4) * 128 + 128],
                                vtring[:, sl, jcc, h * DH:(h + 1) * DH],
                                ptbp[:, jc * 128:(jc + 1) * 128],
                                start=(jc == 0), stop=(jc == nchb - 1))
                        return inst
                    tpv = p.op("tensor", pv, [tpe, tvt[-1]] + first_pv_guard)
                    head_toks.append(tpv)
                    up_te[par] = te2
                    up_ttp[par] = ttp
                    up_tpe[par] = tpe
                    up_tpv[par] = tpv
                    last_tpe_win = tpe
                    unit2 += 1
                tcl = vop(lambda e, qpar=qpar: e.tensor_scalar(
                    out=rcp8[:, qpar, :], in0=sums8[:, qpar, :],
                    scalar1=1e-37, scalar2=None, op0=OP.max), head_toks)
                trc2 = vop(lambda e, qpar=qpar: e.reciprocal(
                    out=rcp8[:, qpar, :], in_=rcp8[:, qpar, :]), [tcl])
                ttr2 = p.op("tensor", lambda e, qpar=qpar: e.transpose(
                    B6[0:8, 0:128], rcp8[:, qpar, :], ident32_sb[:, :]),
                    [trc2] + ([qp[qpar]["trt"]] if qp[qpar]["trt"] else []))
                trt = vop(lambda e, qpar=qpar: e.tensor_copy(
                    out=rT[:, qpar, :], in_=B6[0:8, 0:128]), [ttr2])
                def rball(e, qpar=qpar):
                    inst = None
                    for h in range(HEADS):
                        ob = B4 if h < 4 else B5
                        inst = e.matmul(
                            ob[0:DH, (h % 4) * 128:(h % 4) * 128 + 128],
                            sel8_sb[:, h, :], rT[0:8, qpar, :],
                            start=True, stop=True)
                    return inst
                trb = p.op("tensor", rball,
                           [trt, t_sel8] +
                           ([qp[qpar]["tev"]] if qp[qpar]["tev"] else []))
                rb_toks = [trb] * HEADS
                trbe = vop(lambda e: [e.tensor_copy(
                    out=rbc_sb[0:DH, h * 128:(h + 1) * 128],
                    in_=(B4 if h < 4 else B5)[0:DH,
                                              (h % 4) * 128:
                                              (h % 4) * 128 + 128])
                    for h in range(HEADS)][-1], rb_toks)
                ev2 = []
                for h in range(HEADS):
                    srcb = B2 if h < 4 else B3
                    fc = (h * DH) // 128
                    foff = (h * DH) % 128
                    tev = vop(lambda e, h=h, srcb=srcb, fc=fc,
                              foff=foff, qt_rel=qt_rel: e.tensor_tensor(
                                  out=aout[foff:foff + DH, fc,
                                           qt_rel:qt_rel + 128],
                                  in0=srcb[0:DH, (h % 4) * 128:
                                           (h % 4) * 128 + 128],
                                  in1=rbc_sb[0:DH, h * 128:(h + 1) * 128],
                                  op=OP.mult),
                              [trbe, head_toks[h]])
                    ev2.append(tev)
                qp[qpar] = dict(tcl=tcl, trt=trt, tev=ev2[-1])
                amask_tok[mpar] = tt1
                wo_dep += ev2
                qidx += 1
            if my_qt and dbg and wl == 4 and src is S0:
                tda = p.store(DBG2[4:5, 0:c.INNER * 1],
                              aout.rearrange("p a b -> p (a b)")[0:1,
                              0:c.INNER] if False else aout[0:1, 0, 0:W],
                              wo_dep)
                tdb = p.store(DBG2[5:6, 0:8], sums8[0:1, qidx % 2, :],
                              wo_dep)
                tdc = p.store(DBG2[6:7, 0:BAND], Sm2[0:1, 0:BAND], wo_dep)
                tdd = p.store(DBG2[7:8, 0:BAND],
                              pbf.rearrange("p a b -> p (a b)")[0:1, 0:BAND]
                              if False else pbf[0:1, (qidx - 1) % 2, :],
                              wo_dep)
                wo_dep = wo_dep + [tda, tdb, tdc, tdd]
            if my_qt:
                tr0 = my_qt[0] * 128 - t0w
                trn = len(my_qt) * 128
                wt = None
                prevv = {0: None, 1: None}
                for m in range(NKC):
                    par = m % 2
                    bb = B0 if par == 0 else B1
                    tm = mm_acc(bb[:, 0:trn],
                                [(wo_bf[:, kc, m * 128:(m + 1) * 128],
                                  aout[:, kc, tr0:tr0 + trn])
                                 for kc in range(NQ)],
                                wo_dep + ([prevv[par]] if prevv[par]
                                          else []) + [toks_w[-1]])
                    wt = vop(lambda e, bb=bb, m=m, tr0=tr0, trn=trn, sw=sw:
                             e.tensor_tensor(
                                 out=sw[:, m, 1 + tr0:1 + tr0 + trn],
                                 in0=sw[:, m, 1 + tr0:1 + tr0 + trn],
                                 in1=bb[:, 0:trn], op=OP.add), [tm])
                    prevv[par] = wt
                tst = None
                for kc in range(NKC):
                    tst = p.store(dst[kc, :, t0w:t0w + W],
                                  sw[:, kc, 1:1 + W], [wt])
                win_cons[slot] = tst
                last_store = tst
            else:
                win_cons[slot] = th
        return last_store

    def load_scales(l, after=()):
        t1 = p.load(qse_sb[:, :, :],
                    qse[l, :, :, :].rearrange("a b c -> b a c"), after)
        t2 = p.load(kse_sb[:, :, :],
                    kse[l, :, :, :].rearrange("a b c -> b a c"), [t1])
        return [t1, t2]

    def b_phase(layer, chunks, src, dst, final, after):
        af.reset()
        ab.reset()
        w1_bf = ab.take3(NKC, 2 * HID)
        w2s = ab.take3(2, NH * 128).rearrange("p a (b c) -> p a b c", c=128)
        hn_bf = ab.take3(NKC, CHB)
        ycols = max(NKC * CHB, NH * CHB)
        yreg = ab.take(ycols)
        h2_bf = yreg[:, 0:NKC * CHB].rearrange("p (a b) -> p a b", b=CHB)
        y_bf = yreg[:, 0:NH * CHB].rearrange("p (a b) -> p a b", b=CHB)
        wstage = af.take(HID + 8)
        schunk = af.take3(NKC, 2 * (CHB + 1))
        tmpAB = af.take3(NKC, CHB)
        xmid_b = af.take3(NKC, CHB)
        gg_f = af.take3(2, CHB)

        toks_w = []
        prev = list(after)
        for kc in range(NKC):
            for hh in range(2):
                tl = p.load(wstage[:, 0:HID],
                            w1[layer, kc * 128:(kc + 1) * 128,
                               hh * HID:(hh + 1) * HID], prev)
                prev = [sop(lambda e, kc=kc, hh=hh: e.activation(
                    out=w1_bf[:, kc, hh * HID:(hh + 1) * HID],
                    in_=wstage[:, 0:HID], func=AF.Copy), [tl])]
                toks_w.append(prev[0])
        for m in range(NKC):
            tcast = None
            for kc in range(NH):
                rn = min(128, HID - kc * 128)
                tl = p.load(wstage[0:rn, 0:128],
                            w2[layer, kc * 128:kc * 128 + rn,
                               m * 128:(m + 1) * 128], prev)
                tcast = sop(lambda e, kc=kc, rn=rn: e.activation(
                    out=w2s[0:rn, 0, kc, :], in_=wstage[0:rn, 0:128],
                    func=AF.Copy), [tl])
                prev = [tcast]
            tw2 = p.store(W2C[layer, m, :, :], w2s[:, 0, :, :], [tcast])
            prev = [tw2]
            toks_w.append(tw2)
        t1 = p.load(b1a_sb[:, :], b1a_p[layer, :, :], prev)
        t2 = p.load(b1g_sb[:, :], b1g_p[layer, :, :], [t1])
        t3 = p.load(b2_sb[:, :], b2p[layer, :, :], [t2])
        t4 = p.load(lnbg_sb[:, :, :],
                    lnbg[layer, :, :, :].rearrange("a b c -> b a c"), [t3])
        toks_w += [t1, t2, t3, t4]

        done = [None, None]
        prev_res = None
        prev_y = None
        prev_outst = None
        last = None
        w2buf_cons = {0: None, 1: None}
        for ci, (c0, N) in enumerate(chunks):
            buf = ci % 2
            sc = schunk[:, :, buf * (CHB + 1):buf * (CHB + 1) + N + 1]
            lddep = ([done[buf]] if done[buf] else []) + \
                (list(after) + [toks_w[-1]] if ci == 0 else [])
            tl = None
            for kc in range(NKC):
                tl = p.load(sc[:, kc, :], src[kc, :, c0 - 1:c0 + N],
                            lddep if kc == 0 else [])
            shifted = [sc[:, kc, 1:1 + N] if kc < NKC // 2 else
                       sc[:, kc, 0:N] for kc in range(NKC)]
            pf = ([prev_res] if prev_res else []) + \
                ([prev_y] if prev_y else [])
            st = ln_sums(shifted, hn_bf, h2_bf, N, [tl] + pf, psum_free=pf)
            ap = ln_apply(shifted, hn_bf, tmpAB, N, st,
                          lambda kc: lnbg_sb[0:1, kc, :],
                          lambda kc: lnbg_sb[:, kc, :],
                          2, [tl, toks_w[-1]] + pf)
            ytoks = []
            for m in range(NH):
                a0 = m * 128
                mn = min(128, HID - a0)
                par = m % 2
                ba = B0 if par == 0 else B2
                bg = B1 if par == 0 else B3
                pdep = [ytoks[-2]] if len(ytoks) >= 2 else []
                def w1mm(e, a0=a0, mn=mn, ba=ba, bg=bg, N=N):
                    for i in range(NKC):
                        e.matmul(ba[0:mn, 0:N], w1_bf[:, i, a0:a0 + mn],
                                 hn_bf[:, i, 0:N], start=(i == 0),
                                 stop=(i == NKC - 1))
                    inst = None
                    for i in range(NKC):
                        inst = e.matmul(bg[0:mn, 0:N],
                                        w1_bf[:, i, HID + a0:HID + a0 + mn],
                                        hn_bf[:, i, 0:N], start=(i == 0),
                                        stop=(i == NKC - 1))
                    return inst
                ta = p.op("tensor", w1mm, ap + pdep)
                tg = ta
                tgg = sop(lambda e, bg=bg, m=m, mn=mn, N=N, par=par:
                          e.activation(out=gg_f[0:mn, par, 0:N],
                                       in_=bg[0:mn, 0:N], func=AF.Gelu,
                                       bias=b1g_sb[0:mn, m:m + 1]),
                          [tg, toks_w[-1]] + pdep)
                ty = vop(lambda e, ba=ba, m=m, mn=mn, N=N, par=par:
                         e.scalar_tensor_tensor(
                             out=y_bf[0:mn, m, 0:N], in0=ba[0:mn, 0:N],
                             scalar=b1a_sb[0:mn, m:m + 1],
                             in1=gg_f[0:mn, par, 0:N], op0=OP.add,
                             op1=OP.mult), [tgg, ta])
                ytoks.append(ty)
            prev_y = ytoks[-1]
            res_toks = []
            prevw = {0: None, 1: None}
            for m in range(NKC):
                par = m % 2
                bb = B4 if par == 0 else B5
                gdep = [w2buf_cons[par]] if w2buf_cons[par] else []
                tw = p.load(w2s[:, par, :, :], W2C[layer, m, :, :],
                            gdep + ([toks_w[-1]] if (ci == 0 and m == 0)
                                    else []), eng="gpsimd")
                pairs = []
                for kc in range(NH):
                    rn = min(128, HID - kc * 128)
                    pairs.append((w2s[0:rn, par, kc, :],
                                  y_bf[0:rn, kc, 0:N]))
                tm = mm_acc(bb[:, 0:N], pairs,
                            ytoks + [tw] +
                            ([prevw[par]] if prevw[par] else []))
                w2buf_cons[par] = tm
                tv3 = vop(lambda e, bb=bb, m=m, N=N, sc=sc:
                          e.scalar_tensor_tensor(
                              out=sc[:, m, 1:1 + N], in0=bb[:, 0:N],
                              scalar=b2_sb[:, m:m + 1],
                              in1=sc[:, m, 1:1 + N],
                              op0=OP.add, op1=OP.add), [tm])
                prevw[par] = tv3
                res_toks.append(tv3)
            prev_res = res_toks[-1]
            if not final:
                tst = None
                for kc in range(NKC):
                    tst = p.store(dst[kc, :, c0:c0 + N], sc[:, kc, 1:1 + N],
                                  [res_toks[kc]])
                done[buf] = tst
                last = tst
            else:
                tlx = None
                for kc in range(NKC):
                    tlx = p.load(xmid_b[:, kc, 0:N], XMID[kc, :, c0:c0 + N],
                                 ([prev_outst] if prev_outst else
                                  lddep) if kc == 0 else [])
                tst = None
                for kc in range(NKC):
                    tv4 = vop(lambda e, kc=kc, N=N, sc=sc: e.tensor_tensor(
                        out=xmid_b[:, kc, 0:N], in0=xmid_b[:, kc, 0:N],
                        in1=sc[:, kc, 1:1 + N], op=OP.add),
                        [res_toks[kc], tlx])
                    tst = p.store(OUT[kc, :, c0 - L // 2:c0 - L // 2 + N],
                                  xmid_b[:, kc, 0:N], [tv4])
                done[buf] = tst
                prev_outst = tst
                last = tst
        return last

    a1_last = a_phase(0, c.A1WLO, c.a1qt, attnM1, S0, S0, [p2_done[-1]])
    b1_last = b_phase(0, c.b1chunks, S0, SA, False, [a1_last])
    a2_last = a_phase(1, c.A2WLO, c.a2qt, attnM2, SA, SA, [b1_last])
    b_phase(1, c.b2chunks, SA, None, True, [a2_last])

    emit(nc, p)
    for cm in reversed(ps_ctx + sb_ctx):
        cm.__exit__(None, None, None)
    return nc



def make_core_inputs(cfg, x_b, role, shared):
    c = cfg
    L, D, W = c.L, c.D, c.W
    G0 = 0 if role == 1 else -(L // 2)
    xt = np.ascontiguousarray(x_b.T.astype(np.float32))
    if role == 1:
        xloc = xt
    else:
        xloc = np.concatenate([np.zeros((D, L // 2), np.float32),
                               xt[:, 0:L // 2]], axis=1)
    d = dict(shared)
    d["xT"] = np.ascontiguousarray(xloc.reshape(c.NKC, 128, L))
    g = np.arange(L, dtype=np.int64) + G0
    d["trow"] = np.clip(g, 0, L - 1).astype(np.float32).reshape(1, L)
    units = []
    for ti in range(c.NTS):
        i = c.SIMT0 + ti
        rows = (np.arange(128) + i * 128) + G0
        for cc in range(c.simchunks[ti]):
            cols = (np.arange(512) + cc * 512) + G0
            valid = (cols[None, :] >= 0) & (cols[None, :] < rows[:, None])
            units.append(np.where(valid, 0.0, NEG).astype(ml_dtypes.bfloat16))
    d["simmask"] = np.stack(units)
    selr = np.ones((128, c.NTS), np.float32)
    for ti in range(c.NTS):
        rows = (np.arange(128) + (c.SIMT0 + ti) * 128) + G0
        selr[rows == 0, ti] = 0.0
    d["selmask"] = selr

    def attn_masks(qtiles):
        out = []
        for qt in qtiles:
            qi = (np.arange(128) + qt * 128) + G0
            kj = (np.arange(c.BAND) + qt * 128 - W) + G0
            valid = (kj[None, :] >= 0) & (kj[None, :] <= qi[:, None]) & \
                (qi[:, None] <= kj[None, :] + W)
            out.append(np.where(valid, 0.0, NEG8).astype(ml_dtypes.bfloat16))
        return np.stack(out)
    d["attnM1"] = attn_masks(c.a1qt)
    d["attnM2"] = attn_masks(c.a2qt)
    return d


def make_shared_inputs(cfg, inp):
    c = cfg
    D, DH, NQ, HID, NH, NKC = c.D, c.DH, c.NQ, c.HID, c.NH, c.NKC
    w = {}
    w["emar"] = np.ascontiguousarray(
        np.asarray(inp["ema_out"], np.float32).reshape(NKC, 128, 1))
    w["s_log_tau"] = np.asarray(inp["log_tau"], np.float32).reshape(1, 1)
    w["s_log_blend"] = np.asarray(inp["log_blend"], np.float32).reshape(1, 1)
    w["s_logit_blend_seq"] = np.asarray(
        inp["logit_blend_seq"], np.float32).reshape(1, 1)
    w["wqkv"] = np.ascontiguousarray(np.asarray(inp["wqkv"], np.float32))
    w["wo"] = np.ascontiguousarray(np.asarray(inp["wo"], np.float32))
    w["w1"] = np.ascontiguousarray(np.asarray(inp["w1"], np.float32))
    w["w2"] = np.ascontiguousarray(np.asarray(inp["w2"], np.float32))
    b1 = np.asarray(inp["b1"], np.float32)
    b1a = np.zeros((c.DEPTH, 128, NH), np.float32)
    b1g = np.zeros((c.DEPTH, 128, NH), np.float32)
    for m in range(NH):
        n = min(128, HID - m * 128)
        b1a[:, 0:n, m] = b1[:, m * 128:m * 128 + n]
        b1g[:, 0:n, m] = b1[:, HID + m * 128:HID + m * 128 + n]
    w["b1a_p"] = b1a
    w["b1g_p"] = b1g
    w["b2p"] = np.ascontiguousarray(
        np.asarray(inp["b2"], np.float32).reshape(c.DEPTH, NKC, 128)
        .transpose(0, 2, 1))
    lnbg = np.zeros((c.DEPTH, NKC, 2, 128), np.float32)
    lnbg[:, :, 0, :] = np.asarray(inp["ln_g"], np.float32).reshape(
        c.DEPTH, NKC, 128)
    lnbg[:, :, 1, :] = np.asarray(inp["ln_b"], np.float32).reshape(
        c.DEPTH, NKC, 128)
    w["lnbg"] = lnbg

    def escatter(scale):
        out = np.zeros((c.DEPTH, NQ, 8, 128), np.float32)
        for cc in range(NQ):
            for m in range(128):
                f = cc * 128 + m
                out[:, cc, f // DH, m] = scale[:, f % DH]
        return out
    w["qse"] = escatter(np.asarray(inp["q_scale"], np.float32))
    w["kse"] = escatter(np.asarray(inp["k_scale"], np.float32))
    h8 = np.zeros((NQ, 128, 8), np.float32)
    for cc in range(NQ):
        for m in range(128):
            h8[cc, m, (cc * 128 + m) // DH] = 1.0
    w["h8pat"] = h8.astype(ml_dtypes.bfloat16)
    w["ones1"] = np.ones((1, 128), np.float32)
    w["onecol_bf"] = np.ones((128, 1), np.float32).astype(ml_dtypes.bfloat16)
    w["identity"] = np.eye(128, dtype=np.float32).astype(ml_dtypes.bfloat16)
    w["identity32"] = np.eye(128, dtype=np.float32)
    w["drow"] = (np.arange(128, dtype=np.float32) % (DH // 2)).reshape(128, 1)
    rm = np.zeros((128, 128), np.float32)
    for i in range(128):
        if i % DH < DH // 2:
            rm[i + DH // 2, i] = -1.0
        else:
            rm[i - DH // 2, i] = 1.0
    w["rotm"] = rm
    w["onz"] = np.stack([np.ones(128, np.float32),
                         np.zeros(128, np.float32)])
    w["onesrow"] = np.ones((1, 512), np.float32)
    sel8v = np.zeros((c.HEADS, 8, DH), np.float32)
    for h in range(c.HEADS):
        sel8v[h, h, :] = 1.0
    w["sel8"] = sel8v
    return w


def run(cfg, inp, dbg=False):
    c = cfg
    nc = build_program(c, dbg=dbg)
    shared = make_shared_inputs(c, inp)
    x = np.asarray(inp["x"], np.float32)
    in_maps = []
    for b in range(c.NB):
        for role in (0, 1):
            in_maps.append(make_core_inputs(c, x[b], role, shared))
    res = run_bass_kernel_spmd(nc, in_maps, core_ids=list(range(2 * c.NB)))
    if dbg:
        return res
    outs = []
    for b in range(c.NB):
        rb = np.asarray(res.results[2 * b]["OUT"]).reshape(c.D, c.L // 2)
        ra = np.asarray(res.results[2 * b + 1]["OUT"]).reshape(c.D, c.L // 2)
        outs.append(np.concatenate([rb, ra], axis=1).T)
    return np.ascontiguousarray(np.stack(outs).astype(np.float32))


def kernel(L0_attn_x, L0_attn_ema_out, L0_attn_logit_decay, L0_attn_log_tau,
           L0_attn_log_blend, L0_attn_logit_blend_seq, L0_ffn_wqkv,
           L0_ffn_q_scale, L0_ffn_k_scale, L0_ffn_wo, L0_ffn_ln_g,
           L0_ffn_ln_b, L0_ffn_w1, L0_ffn_b1, L0_ffn_w2, L0_ffn_b2):
    inp = dict(x=L0_attn_x, ema_out=L0_attn_ema_out, log_tau=L0_attn_log_tau,
               log_blend=L0_attn_log_blend,
               logit_blend_seq=L0_attn_logit_blend_seq, wqkv=L0_ffn_wqkv,
               q_scale=L0_ffn_q_scale, k_scale=L0_ffn_k_scale, wo=L0_ffn_wo,
               ln_g=L0_ffn_ln_g, ln_b=L0_ffn_ln_b, w1=L0_ffn_w1,
               b1=L0_ffn_b1, w2=L0_ffn_w2, b2=L0_ffn_b2)
    return run(REAL, inp)
